# version 01 (final)
# baseline (speedup 1.0000x reference)
import numpy as np
import ml_dtypes
import concourse.bass as bass
import concourse.mybir as mybir
from concourse.bass_utils import run_bass_kernel_spmd

F32 = mybir.dt.float32
BF16 = mybir.dt.bfloat16
AF = mybir.ActivationFunctionType
OP = mybir.AluOpType
AX = mybir.AxisListType

Bsz, Lseq, DIM = 4, 4096, 4096
T = 2048
H, P, G, N, CS, KCONV = 128, 64, 8, 128, 128, 4
INTER = H * P
CONV = INTER + 2 * G * N
NPROJ = INTER + CONV + H
NBLK = NPROJ // 128
NCH = 16
NEG = -1e30
SK_ITERS = 50
SK_TAU = 0.1
RMS_EPS = 1.1920929e-07
LN_EPS = 1e-5

GATE_BLKS = list(range(0, 64))
HBC_BLKS = list(range(64, 144))
DT_BLK = 144
RG_PAIRS = [[0, 1], [2, 3], [4, 5], [6, 7]]

_STOP = None
_TAPS = ()



import contextlib as _ctx


@_ctx.contextmanager
def perm_sem(nc, name):
    yield nc.alloc_semaphore(name=name)


def chain(nc, name, ops):
    kinds = ["g" if (d and en == "gpsimd") else "m" for en, d, _ in ops]
    incs = [16 if d else 1 for (_, d, _) in ops]
    prefm = [0]
    prefg = [0]
    for k, kd in zip(incs, kinds):
        prefm.append(prefm[-1] + (k if kd == "m" else 0))
        prefg.append(prefg[-1] + (k if kd == "g" else 0))
    with perm_sem(nc, name) as sem, perm_sem(nc, name + "g") as semg, \
         nc.Block() as blk:
        for eng in ("sync", "vector", "scalar", "tensor", "gpsimd"):
            mine = [(i, f) for i, (en, d, f) in enumerate(ops) if en == eng]
            if not mine:
                continue

            def body(e, mine=mine):
                for i, f in mine:
                    if prefm[i] > 0:
                        e.wait_ge(sem, prefm[i])
                    if prefg[i] > 0:
                        e.wait_ge(semg, prefg[i])
                    f(e).then_inc(sem if kinds[i] == "m" else semg, incs[i])
            getattr(blk, eng)(body)



def host_prepare(inputs):
    f32 = np.float32
    d = {}
    A = np.ascontiguousarray(np.asarray(inputs["L0_attn_in_proj_w"]), dtype=f32)
    d["w_in"] = np.ascontiguousarray(
        A.reshape(NBLK, 128, 32, 128).transpose(0, 3, 2, 1)).reshape(NBLK, 128, 4096)
    W = np.ascontiguousarray(np.asarray(inputs["L0_attn_out_proj_w"]), dtype=f32)
    d["w_out"] = np.ascontiguousarray(
        W.reshape(32, 128, 64, 128).transpose(0, 3, 2, 1)).reshape(32, 128, 8192)
    d["w_br"] = np.ascontiguousarray(
        np.asarray(inputs["L0_ffn_W_branch"], dtype=f32)).reshape(8, 128, 1024)
    Wcat = np.concatenate([inputs["L0_ffn_W_res"], inputs["L0_ffn_W_pre"],
                           inputs["L0_ffn_W_post"]], axis=0).astype(f32)
    d["w24"] = np.ascontiguousarray(Wcat.T).reshape(32, 128, 24)
    d["base24"] = np.concatenate(
        [np.asarray(inputs["L0_ffn_H_res_base"]).reshape(16),
         np.asarray(inputs["L0_ffn_H_pre_base"]),
         np.asarray(inputs["L0_ffn_H_post_base"])]).astype(f32).reshape(24, 1)
    selb = np.zeros((24, 24 * 128), dtype=f32)
    for r in range(24):
        selb[r, r * 128:(r + 1) * 128] = 1.0
    d["selb"] = selb
    cw = np.asarray(inputs["L0_attn_conv_w"], dtype=f32)
    convd = np.zeros((80, KCONV, 128, 128), dtype=f32)
    idx = np.arange(128)
    for b in range(80):
        for k in range(KCONV):
            convd[b, k, idx, idx] = cw[b * 128:(b + 1) * 128, 0, k]
    d["convd"] = convd
    d["convb"] = np.ascontiguousarray(
        np.asarray(inputs["L0_attn_conv_b"], dtype=f32).reshape(80, 128).T)
    d["dtb"] = np.asarray(inputs["L0_attn_dt_bias"], dtype=f32).reshape(128, 1)
    d["alog"] = np.asarray(inputs["L0_attn_A_log"], dtype=f32).reshape(128, 1)
    Dv = np.asarray(inputs["L0_attn_D"], dtype=f32)
    dcol = np.empty((128, 64), dtype=f32)
    dcol[:64, :] = Dv[0::2][None, :]
    dcol[64:, :] = Dv[1::2][None, :]
    d["dcol"] = dcol
    d["nrmw"] = np.ascontiguousarray(
        np.asarray(inputs["L0_attn_norm_w"], dtype=f32).reshape(64, 128).T)
    d["gatep"] = np.asarray(inputs["L0_ffn_gate"], dtype=f32).reshape(1, 1)
    tri = np.where(np.arange(128)[None, :] >= np.arange(128)[:, None], 0.0, NEG)
    d["tri"] = np.ascontiguousarray(tri.astype(f32))
    d["ident"] = np.eye(128, dtype=f32)
    x0 = np.asarray(inputs["L0_attn_hidden_states"], dtype=f32)
    d["_x0t"] = [np.ascontiguousarray(x0[c // 2, (c % 2) * T:(c % 2 + 1) * T, :].T)
                 for c in range(8)]
    return d



_RACECHECK = True


def build_program():
    nc = bass.Bass("TRN2", target_bir_lowering=False, debug=False,
                   detect_race_conditions=_RACECHECK)
    t = {}

    def din(name, shape, dt=F32):
        t[name] = nc.dram_tensor(name, shape, dt, kind="ExternalInput").ap()

    def dint(name, shape, dt):
        t[name] = nc.dram_tensor(name, shape, dt).ap()

    din("x0t", [DIM, T])
    din("w_in", [NBLK, 128, 4096])
    din("w_out", [32, 128, 8192])
    din("w_br", [8, 128, 1024])
    din("w24", [32, 128, 24])
    din("base24", [24, 1])
    din("selb", [24, 24 * 128])
    din("convd", [80, KCONV, 128, 128])
    din("convb", [128, 80])
    din("dtb", [128, 1])
    din("alog", [128, 1])
    din("dcol", [128, 64])
    din("nrmw", [128, 64])
    din("gatep", [1, 1])
    din("tri", [128, 128])
    din("ident", [128, 128])
    din("oddm", [128, 1])
    t["outt"] = nc.dram_tensor("outt", [DIM, T], F32, kind="ExternalOutput").ap()

    dint("x1_sp", [DIM, T], F32)
    dint("x2_sp", [DIM, T], F32)
    dint("xln_sp", [DIM, T], BF16)
    dint("hbc_sp", [CONV, T], BF16)
    dint("xpl_sp", [INTER, T], BF16)
    dint("bpl_sp", [G * N, T], BF16)
    dint("cpl_sp", [G * N, T], BF16)
    dint("gpl_sp", [INTER, T], BF16)
    dint("s_sp", [NCH, 128, INTER], BF16)
    dint("y_sp", [64, 128, T], BF16)
    dint("cc1i", [128, 240], F32)
    dint("cc1o", [256, 240], F32)
    dint("cc2i", [128, INTER], F32)
    dint("cc2o", [256, INTER], F32)
    dint("csr_sp", [16, 128], F32)
    dint("flt_sp", [64, 128], F32)
    for tp in _TAPS:
        t["tap_" + tp[0]] = nc.dram_tensor(
            "tap_" + tp[0], list(tp[1]), tp[2] if len(tp) > 2 else F32,
            kind="ExternalOutput").ap()

    with (
        nc.sbuf_tensor("ident_f", [128, 128], F32) as ident_f,
        nc.sbuf_tensor("tri_f", [128, 128], F32) as tri_f,
        nc.sbuf_tensor("ones_r_f", [1, 128], F32) as ones_r_f,
        nc.sbuf_tensor("ones_r_b", [1, 128], BF16) as ones_r_b,
        nc.sbuf_tensor("ones_c_b", [128, 1], BF16) as ones_c_b,
        nc.sbuf_tensor("ones_t_f", [128, 128], F32) as ones_t_f,
        nc.sbuf_tensor("w24_sb", [128, 32 * 24], BF16) as w24_sb,
        nc.sbuf_tensor("selb_sb", [24, 24 * 128], BF16) as selb_sb,
        nc.sbuf_tensor("base24_sb", [24, 1], F32) as base24_sb,
        nc.sbuf_tensor("convb_sb", [128, 80], F32) as convb_sb,
        nc.sbuf_tensor("dtb_sb", [128, 1], F32) as dtb_sb,
        nc.sbuf_tensor("nega_sb", [128, 1], F32) as nega_sb,
        nc.sbuf_tensor("dcol_sb", [128, 64], F32) as dcol_sb,
        nc.sbuf_tensor("nrmw_sb", [128, 64], F32) as nrmw_sb,
        nc.sbuf_tensor("oddm_sb", [128, 1], F32) as oddm_sb,
        nc.sbuf_tensor("gcol_sb", [128, 1], F32) as gcol_sb,
        nc.sbuf_tensor("g1col_sb", [128, 1], F32) as g1col_sb,
    ):
        C = dict(ident_f=ident_f, tri_f=tri_f, ones_r_f=ones_r_f,
                 ones_r_b=ones_r_b, ones_c_b=ones_c_b, ones_t_f=ones_t_f,
                 w24_sb=w24_sb, selb_sb=selb_sb, base24_sb=base24_sb,
                 convb_sb=convb_sb, dtb_sb=dtb_sb,
                 nega_sb=nega_sb, dcol_sb=dcol_sb, nrmw_sb=nrmw_sb,
                 oddm_sb=oddm_sb, gcol_sb=gcol_sb, g1col_sb=g1col_sb)
        _emit_all(nc, t, C)
    return nc


def _tap(t, name):
    return t.get("tap_" + name)


def _emit_all(nc, t, C):
    phase_setup(nc, t, C)
    if _STOP == "setup":
        return
    phase_ffn(nc, t, C, t["x0t"], t["x1_sp"], "f1")
    if _tap(t, "x1") is not None:
        dump_dram(nc, t["x1_sp"], _tap(t, "x1"))
    if _STOP == "ffn1":
        return
    with nc.sbuf_tensor("dtraw", [128, T], F32) as dtraw:
        with nc.sbuf_tensor("xln2", [128, 32 * T], BF16) as xln2:
            phase_ln2(nc, t, C, xln2)
            if _tap(t, "xln2") is not None:
                dump_sbuf(nc, xln2, _tap(t, "xln2"), 32)
            if _STOP == "ln2":
                return
            phase_inproj(nc, t, C, xln2, dtraw)
        if _tap(t, "hbc") is not None:
            dump_dram(nc, t["hbc_sp"], _tap(t, "hbc"))
        if _tap(t, "gpl") is not None:
            dump_dram(nc, t["gpl_sp"], _tap(t, "gpl"))
        if _tap(t, "dtraw") is not None:
            dump_sbuf(nc, dtraw, _tap(t, "dtraw"), 1)
        if _STOP == "inproj":
            return
        phase_conv(nc, t, C)
        if _tap(t, "xpl") is not None:
            dump_dram(nc, t["xpl_sp"], _tap(t, "xpl"))
        if _tap(t, "bpl") is not None:
            dump_dram(nc, t["bpl_sp"], _tap(t, "bpl"))
        if _tap(t, "cpl") is not None:
            dump_dram(nc, t["cpl_sp"], _tap(t, "cpl"))
        if _STOP == "conv":
            return
        with (
            nc.sbuf_tensor("cum_fm", [128, T], F32) as cum_fm,
            nc.sbuf_tensor("dt_tok", [128, T], F32) as dt_tok,
            nc.sbuf_tensor("cum_tok", [128, T], F32) as cum_tok,
            nc.sbuf_tensor("csrows", [1, T], F32) as csrows,
            nc.sbuf_tensor("ecsrows", [1, T], F32) as ecsrows,
            nc.sbuf_tensor("state", [128, INTER], F32) as state,
        ):
            S = dict(cum_fm=cum_fm, dt_tok=dt_tok, cum_tok=cum_tok,
                     csrows=csrows, ecsrows=ecsrows, state=state)
            phase_ssd_prep(nc, t, C, dtraw, S)
            if _tap(t, "cum") is not None:
                dump_sbuf(nc, cum_fm, _tap(t, "cum"), 1)
            if _STOP == "prep":
                return
            phase_ssd_pass1(nc, t, C, S)
            if _tap(t, "s0") is not None:
                dump_dram(nc, t["s_sp"][0], _tap(t, "s0"))
            if _tap(t, "einit") is not None:
                dump_sbuf(nc, state, _tap(t, "einit"), 1)
            if _STOP == "pass1":
                return
            phase_ssd_pass2(nc, t, C, S)
    if _tap(t, "ysp") is not None:
        dump_dram(nc, t["y_sp"].rearrange("c p l -> (c p) l"), _tap(t, "ysp"))
    if _STOP == "pass2":
        return
    phase_outproj(nc, t, C)
    if _tap(t, "x2") is not None:
        dump_dram(nc, t["x2_sp"], _tap(t, "x2"))
    if _STOP == "outproj":
        return
    phase_ffn(nc, t, C, t["x2_sp"], t["outt"], "f2")


_DUMP_SEM = {}


def _dump_sem(nc):
    if id(nc) not in _DUMP_SEM:
        _DUMP_SEM[id(nc)] = [nc.alloc_semaphore(name="dumps"), 0]
    return _DUMP_SEM[id(nc)]


def dump_dram(nc, src, dst):
    rows, cols = src.shape
    dt = src.dtype
    sh = _dump_sem(nc)
    with (
        nc.sbuf_tensor(f"dmp_{dst.tensor.name}", [128, cols], dt) as tile,
        nc.Block() as blk,
    ):
        s = sh[0]
        @blk.sync
        def _(e):
            n = sh[1]
            for r0 in range(0, rows, 128):
                r1 = min(r0 + 128, rows)
                e.dma_start(out=tile[0:r1 - r0, :], in_=src[r0:r1, :]).then_inc(s, 16)
                n += 16
                e.wait_ge(s, n)
                e.dma_start(out=dst[r0:r1, :], in_=tile[0:r1 - r0, :]).then_inc(s, 16)
                n += 16
                e.wait_ge(s, n)
            sh[1] = n


def dump_sbuf(nc, src, dst, nchunks):
    cols = src.shape[1] // nchunks
    sh = _dump_sem(nc)
    with nc.Block() as blk:
        s = sh[0]

        @blk.sync
        def _(e):
            n = sh[1]
            for c in range(nchunks):
                e.dma_start(out=dst[c * 128:(c + 1) * 128, :],
                            in_=src[:, c * cols:(c + 1) * cols]).then_inc(s, 16)
                n += 16
            e.wait_ge(s, n)
            sh[1] = n



def phase_setup(nc, t, C):
    with (
        nc.sbuf_tensor("su_al", [128, 1], F32) as al,
        nc.sbuf_tensor("su_g", [1, 1], F32) as g0,
        nc.sbuf_tensor("su_g2", [1, 1], F32) as g1,
        nc.psum_tensor("su_p", [128, 1], F32) as pb,
    ):
        ops = []
        for name, dst in [("ident", C["ident_f"]), ("tri", C["tri_f"]),
                          ("base24", C["base24_sb"]), ("convb", C["convb_sb"]),
                          ("dtb", C["dtb_sb"]), ("dcol", C["dcol_sb"]),
                          ("nrmw", C["nrmw_sb"]), ("oddm", C["oddm_sb"])]:
            ops.append(("sync", True,
                        lambda e, n=name, d=dst: e.dma_start(out=d[:], in_=t[n][:, :])))
        ops.append(("sync", True, lambda e: e.dma_start(out=g0[:], in_=t["gatep"][:, :])))
        ops.append(("sync", True, lambda e: e.dma_start(out=al[:], in_=t["alog"][:, :])))
        ops.append(("gpsimd", True, lambda e: e.dma_start(
            out=C["w24_sb"][:].rearrange("p (k j) -> p k j", j=24),
            in_=t["w24"][:, :, :].rearrange("k p j -> p k j"))))
        ops.append(("gpsimd", True, lambda e: e.dma_start(
            out=C["selb_sb"][:], in_=t["selb"][:, :])))
        ops.append(("vector", False, lambda e: e.memset(C["ones_r_f"][:], 1.0)))
        ops.append(("vector", False, lambda e: e.memset(C["ones_r_b"][:], 1.0)))
        ops.append(("vector", False, lambda e: e.memset(C["ones_c_b"][:], 1.0)))
        ops.append(("vector", False, lambda e: e.memset(C["ones_t_f"][:], 1.0)))
        ops.append(("scalar", False, lambda e: e.activation(al[:], al[:], AF.Exp)))
        ops.append(("vector", False, lambda e: e.tensor_scalar(
            C["nega_sb"][:], al[:], -1.0, None, OP.mult)))
        ops.append(("scalar", False, lambda e: e.activation(g1[:], g0[:], AF.Sigmoid)))
        ops.append(("tensor", False, lambda e: e.matmul(
            pb[:], C["ones_r_f"][:], g1[:], start=True, stop=True)))
        ops.append(("vector", False, lambda e: e.tensor_scalar(
            C["gcol_sb"][:], pb[:], 0.5, None, OP.mult)))
        ops.append(("vector", False, lambda e: e.tensor_scalar(
            C["g1col_sb"][:], pb[:], -0.5, 0.5, OP.mult, OP.add)))
        chain(nc, "setup", ops)



def phase_ffn(nc, t, C, xin, xout, tag):
    import contextlib
    with contextlib.ExitStack() as es:
        def sb(name, shape, dt=F32):
            return es.enter_context(nc.sbuf_tensor(f"{tag}{name}", shape, dt))
        m_row = sb("_m", [1, T])
        v_row = sb("_v", [1, T])
        r_row = sb("_r", [1, T])
        mr_row = sb("_mr", [1, T])
        ffs_row = sb("_ff", [1, T])
        lg_sb = sb("_lg", [24, T])
        ztok = sb("_zt", [128, 16 * 24])
        hx = sb("_hx", [128, 16])
        hpre_tok = sb("_hp", [128, 16 * 4])
        hpost_tok = sb("_hq", [128, 16 * 4])
        P_sk = sb("_P", [128, 16 * 16])
        a_sk = sb("_aa", [128, 16 * 4])
        b_sk = sb("_bb", [128, 16 * 4])
        t_sk = sb("_tt", [128, 16 * 16])
        hpre_fm = sb("_hpf", [4, T], BF16)
        h_fm = sb("_hf", [24, T], BF16)
        y_sb = sb("_y", [128, 8 * T], BF16)
        with nc.sbuf_tensor(f"{tag}_hprt", [128, 4 * T], BF16) as hpre_t, \
             nc.sbuf_tensor(f"{tag}_xpre", [128, 8 * T], BF16) as xpre:
            with (
                nc.sbuf_tensor(f"{tag}_rb", [128, T], F32) as r_sb,
                nc.sbuf_tensor(f"{tag}_mrb", [128, T], F32) as mr_sb,
                nc.psum_tensor(f"{tag}_ps", [128, T], F32) as ps,
                nc.psum_tensor(f"{tag}_pl", [24, T], F32) as plog,
            ):
                _ffn_A1(nc, t, C, xin, tag, m_row, v_row, r_row, mr_row,
                        ffs_row, r_sb, mr_sb, ps)
                _ffn_A2(nc, t, C, xin, tag, r_sb, mr_sb, plog)
                _ffn_A2post(nc, t, C, tag, ffs_row, plog, ps, lg_sb, ztok, hx,
                            hpre_tok, hpost_tok, P_sk, b_sk, hpre_fm)
                _ffn_Bpre(nc, t, C, tag, ps, hpre_fm, hpre_t)
            _ffn_Bloop(nc, t, C, tag, hpre_t, xpre, P_sk, a_sk, b_sk, t_sk)
            _ffn_C(nc, t, C, tag, xpre, y_sb, P_sk, a_sk, b_sk, t_sk,
                   hpre_tok, hpost_tok, ztok, h_fm)
        _ffn_D(nc, t, C, xin, xout, tag, h_fm, y_sb)


def _ffn_A1(nc, t, C, xin, tag, m_row, v_row, r_row, mr_row, ffs_row,
            r_sb, mr_sb, ps):
    with (
        nc.sbuf_tensor(f"{tag}A_x0", [128, 2 * T], F32) as x0b,
        nc.sbuf_tensor(f"{tag}A_xb", [128, 2 * T], BF16) as xbb,
        nc.sbuf_tensor(f"{tag}A_sq", [128, 2 * T], BF16) as sqb,
        perm_sem(nc, f"{tag}A_ld") as ld,
        perm_sem(nc, f"{tag}A_cv") as cv,
        perm_sem(nc, f"{tag}A_sq") as sq,
        perm_sem(nc, f"{tag}A_mm") as mm,
        nc.Block() as blk,
    ):
        @blk.sync
        def _(e):
            for c in range(32):
                if c >= 2:
                    e.wait_ge(cv, c - 1)
                    e.wait_ge(sq, c - 1)
                e.dma_start(out=x0b[:, (c % 2) * T:(c % 2 + 1) * T],
                            in_=xin[c * 128:(c + 1) * 128, :]).then_inc(ld, 16)

        @blk.vector
        def _(e):
            for c in range(32):
                e.wait_ge(ld, (c + 1) * 16)
                if c >= 2:
                    e.wait_ge(mm, c - 1)
                sl = slice((c % 2) * T, (c % 2 + 1) * T)
                e.tensor_copy(xbb[:, sl], x0b[:, sl]).then_inc(cv, 1)

        @blk.scalar
        def _(e):
            for c in range(32):
                e.wait_ge(ld, (c + 1) * 16)
                if c >= 2:
                    e.wait_ge(mm, c - 1)
                sl = slice((c % 2) * T, (c % 2 + 1) * T)
                e.activation(sqb[:, sl], x0b[:, sl], AF.Square).then_inc(sq, 1)

        @blk.tensor
        def _(e):
            for c in range(32):
                e.wait_ge(cv, c + 1)
                e.wait_ge(sq, c + 1)
                o = (c % 2) * T
                st, sp = c == 0, c == 31
                last = None
                for tg in range(4):
                    s = slice(tg * 512, (tg + 1) * 512)
                    e.matmul(ps[0:1, s], C["ones_c_b"][:],
                             xbb[:, o + tg * 512:o + (tg + 1) * 512], start=st, stop=sp)
                    last = e.matmul(ps[32:33, s], C["ones_c_b"][:],
                                    sqb[:, o + tg * 512:o + (tg + 1) * 512],
                                    start=st, stop=sp)
                last.then_inc(mm, 1)

    ops = [
        ("vector", False, lambda e: e.tensor_scalar(m_row[:], ps[0:1, :], 1.0 / DIM, None, OP.mult)),
        ("vector", False, lambda e: e.tensor_scalar(v_row[:], ps[32:33, :], 1.0 / DIM, None, OP.mult)),
        ("vector", False, lambda e: e.tensor_mul(r_row[:], m_row[:], m_row[:])),
        ("vector", False, lambda e: e.tensor_sub(v_row[:], v_row[:], r_row[:])),
        ("vector", False, lambda e: e.tensor_scalar(mr_row[:], v_row[:], LN_EPS, None, OP.add)),
        ("scalar", False, lambda e: e.activation(r_row[:], mr_row[:], AF.Ln)),
        ("scalar", False, lambda e: e.activation(r_row[:], r_row[:], AF.Exp, scale=-0.5)),
        ("vector", False, lambda e: e.reciprocal(ffs_row[:], mr_row[:])),
        ("vector", False, lambda e: e.tensor_mul(ffs_row[:], ffs_row[:], v_row[:])),
        ("vector", False, lambda e: e.tensor_scalar(ffs_row[:], ffs_row[:], RMS_EPS, None, OP.add)),
        ("scalar", False, lambda e: e.activation(ffs_row[:], ffs_row[:], AF.Ln)),
        ("scalar", False, lambda e: e.activation(ffs_row[:], ffs_row[:], AF.Exp, scale=-0.5)),
        ("vector", False, lambda e: e.tensor_mul(mr_row[:], m_row[:], r_row[:])),
        ("tensor", False, lambda e: _bcast_full(e, C, ps, r_row)),
        ("vector", False, lambda e: e.tensor_copy(r_sb[:], ps[0:128, :])),
        ("tensor", False, lambda e: _bcast_full(e, C, ps, mr_row)),
        ("vector", False, lambda e: e.tensor_copy(mr_sb[:], ps[0:128, :])),
    ]
    chain(nc, f"{tag}A1p", ops)


def _bcast_full(e, C, ps, row):
    last = None
    for tg in range(4):
        s = slice(tg * 512, (tg + 1) * 512)
        last = e.matmul(ps[0:128, s], C["ones_r_f"][:], row[0:1, s],
                        start=True, stop=True)
    return last


def _ffn_A2(nc, t, C, xin, tag, r_sb, mr_sb, plog):
    with (
        nc.sbuf_tensor(f"{tag}B_x0", [128, 2 * T], F32) as x0b,
        nc.sbuf_tensor(f"{tag}B_t", [128, 2 * T], F32) as tmpb,
        nc.sbuf_tensor(f"{tag}B_xl", [128, 2 * T], BF16) as xlb,
        perm_sem(nc, f"{tag}B_ld") as ld,
        perm_sem(nc, f"{tag}B_cv") as cv,
        perm_sem(nc, f"{tag}B_mm") as mm,
        perm_sem(nc, f"{tag}B_st") as stS,
        nc.Block() as blk,
    ):
        @blk.sync
        def _(e):
            for c in range(32):
                if c >= 2:
                    e.wait_ge(cv, 2 * (c - 1))
                e.dma_start(out=x0b[:, (c % 2) * T:(c % 2 + 1) * T],
                            in_=xin[c * 128:(c + 1) * 128, :]).then_inc(ld, 16)

        @blk.vector
        def _(e):
            for c in range(32):
                e.wait_ge(ld, (c + 1) * 16)
                if c >= 2:
                    e.wait_ge(mm, c - 1)
                    e.wait_ge(stS, (c - 1) * 16)
                sl = slice((c % 2) * T, (c % 2 + 1) * T)
                e.tensor_mul(tmpb[:, sl], x0b[:, sl], r_sb[:]).then_inc(cv, 1)
                e.wait_ge(cv, 2 * c + 1)
                e.tensor_sub(xlb[:, sl], tmpb[:, sl], mr_sb[:]).then_inc(cv, 1)

        @blk.scalar
        def _(e):
            for c in range(32):
                e.wait_ge(cv, 2 * (c + 1))
                e.dma_start(out=t["xln_sp"][c * 128:(c + 1) * 128, :],
                            in_=xlb[:, (c % 2) * T:(c % 2 + 1) * T]).then_inc(stS, 16)

        @blk.tensor
        def _(e):
            for c in range(32):
                e.wait_ge(cv, 2 * (c + 1))
                o = (c % 2) * T
                last = None
                for tg in range(4):
                    s = slice(tg * 512, (tg + 1) * 512)
                    last = e.matmul(plog[:, s], C["w24_sb"][:, c * 24:(c + 1) * 24],
                                    xlb[:, o + tg * 512:o + (tg + 1) * 512],
                                    start=(c == 0), stop=(c == 31))
                last.then_inc(mm, 1)


def _ffn_A2post(nc, t, C, tag, ffs_row, plog, ps, lg_sb, ztok, hx,
                hpre_tok, hpost_tok, P_sk, b_sk, hpre_fm):
    z3 = ztok[:].rearrange("p (s k) -> p s k", k=24)
    ops = [
        ("scalar", False, lambda e: e.copy(lg_sb[:], plog[:, :])),
        ("tensor", False, lambda e: _bcast24(e, C, ps, ffs_row)),
        ("vector", False, lambda e: e.tensor_mul(lg_sb[:], lg_sb[:], ps[0:24, :])),
        ("vector", False, lambda e: e.tensor_scalar(
            lg_sb[:], lg_sb[:], C["base24_sb"][:], None, OP.add)),
        ("tensor", False, lambda e: _t_rows_to_tok(e, C, ps, lg_sb, 24)),
        ("vector", False, lambda e: _copy_tok_slabs(e, ztok, ps, 24)),
    ]
    for dst, off in ((hpre_tok, 16), (hpost_tok, 20)):
        d3 = dst[:].rearrange("p (s k) -> p s k", k=4)
        zin = z3[:, :, off:off + 4]
        ops += [
            ("vector", False, lambda e, zin=zin: e.tensor_reduce(
                hx[:], zin, AX.X, OP.max)),
            ("vector", False, lambda e, zin=zin, d3=d3: e.tensor_tensor(
                d3, zin, hx[:].unsqueeze(2).broadcast_to([128, 16, 4]),
                OP.subtract)),
            ("scalar", False, lambda e, dst=dst: e.activation(dst[:], dst[:], AF.Exp)),
            ("vector", False, lambda e, d3=d3: e.tensor_reduce(hx[:], d3, AX.X, OP.add)),
            ("vector", False, lambda e: e.reciprocal(hx[:], hx[:])),
            ("vector", False, lambda e, d3=d3: e.tensor_tensor(
                d3, d3, hx[:].unsqueeze(2).broadcast_to([128, 16, 4]), OP.mult)),
        ]
    ops += [
        ("scalar", False, lambda e: e.activation(
            P_sk[:].rearrange("p (s k) -> p s k", k=16), z3[:, :, 0:16],
            AF.Exp, scale=1.0 / SK_TAU)),
        ("vector", False, lambda e: e.memset(b_sk[:], 1.0)),
        ("tensor", False, lambda e: _t_tok_to_rows(e, C, plog, hpre_tok, 4)),
        ("vector", False, lambda e: e.tensor_copy(hpre_fm[:], plog[0:4, :])),
    ]
    chain(nc, f"{tag}A2p", ops)


def _bcast24(e, C, ps, row):
    last = None
    for tg in range(4):
        s = slice(tg * 512, (tg + 1) * 512)
        last = e.matmul(ps[0:24, s], C["ones_r_f"][0:1, 0:24], row[0:1, s],
                        start=True, stop=True)
    return last


def _t_rows_to_tok(e, C, pdst, src_fm, k):
    last = None
    for sl in range(16):
        last = e.transpose(pdst[0:128, sl * k:(sl + 1) * k],
                           src_fm[0:k, sl * 128:(sl + 1) * 128],
                           C["ident_f"][0:k, 0:k])
    return last


def _t_tok_to_rows(e, C, pdst, src_tok, k):
    last = None
    for sl in range(16):
        last = e.transpose(pdst[0:k, sl * 128:(sl + 1) * 128],
                           src_tok[:, sl * k:(sl + 1) * k], C["ident_f"][:])
    return last


def _copy_tok_slabs(e, dst, psrc, k):
    last = None
    for sl in range(16):
        last = e.tensor_copy(dst[:, sl * k:(sl + 1) * k],
                             psrc[0:128, sl * k:(sl + 1) * k])
    return last


def _ffn_Bpre(nc, t, C, tag, ps, hpre_fm, hpre_t):
    ops = []
    for j in range(4):
        ops.append(("tensor", False, lambda e, j=j: _bcast_sel(
            e, C, ps, hpre_fm, 4, j)))
        ops.append(("vector", False, lambda e, j=j: e.tensor_copy(
            hpre_t[:, j * T:(j + 1) * T], ps[0:128, :])))
    chain(nc, f"{tag}Bb", ops)


def _ffn_Bloop(nc, t, C, tag, hpre_t, xpre, P_sk, a_sk, b_sk, t_sk):
    with (
        nc.sbuf_tensor(f"{tag}C_xl", [128, 2 * T], BF16) as xlb,
        nc.sbuf_tensor(f"{tag}C_tmp", [128, T], BF16) as tmp,
        perm_sem(nc, f"{tag}C_ld") as ld,
        perm_sem(nc, f"{tag}C_pp") as pp,
        perm_sem(nc, f"{tag}C_sk") as sk,
        nc.Block() as blk,
    ):
        @blk.sync
        def _(e):
            i = 0
            for c8 in range(8):
                for j in range(4):
                    if i >= 2:
                        e.wait_ge(pp, 2 * (i - 1))
                    ch = j * 8 + c8
                    e.dma_start(out=xlb[:, (i % 2) * T:(i % 2 + 1) * T],
                                in_=t["xln_sp"][ch * 128:(ch + 1) * 128, :]
                                ).then_inc(ld, 16)
                    i += 1

        @blk.gpsimd
        def _(e):
            i = 0
            for c8 in range(8):
                xp = xpre[:, c8 * T:(c8 + 1) * T]
                for j in range(4):
                    e.wait_ge(ld, (i + 1) * 16)
                    sl = slice((i % 2) * T, (i % 2 + 1) * T)
                    if j == 0:
                        e.tensor_mul(xp, xlb[:, sl], hpre_t[:, 0:T]).then_inc(pp, 1)
                        e.engine_nop().then_inc(pp, 1)
                    else:
                        e.tensor_mul(tmp[:], xlb[:, sl],
                                     hpre_t[:, j * T:(j + 1) * T]).then_inc(pp, 1)
                        e.wait_ge(pp, 2 * i + 1)
                        e.tensor_add(xp, xp, tmp[:]).then_inc(pp, 1)
                    i += 1

        @blk.vector
        def _(e):
            b2 = b_sk[:].rearrange("p (s k) -> p s k", k=4)
            a2 = a_sk[:].rearrange("p (s k) -> p s k", k=4)
            P3 = P_sk[:].rearrange("p (s i j) -> p s i j", i=4, j=4)
            t3 = t_sk[:].rearrange("p (s i j) -> p s i j", i=4, j=4)
            t3T = t_sk[:].rearrange("p (s i j) -> p s i j", i=4, j=4
                                    ).transpose([0, 1, 3, 2])
            n = 0
            for it in range(SK_ITERS):
                e.tensor_tensor(t3, P3, b_sk[:].rearrange(
                    "p (s j) -> p s j", j=4).unsqueeze(2)
                    .broadcast_to([128, 16, 4, 4]), OP.mult).then_inc(sk, 1)
                n += 1
                e.wait_ge(sk, n)
                e.tensor_reduce(a2, t3, AX.X, OP.add).then_inc(sk, 1)
                n += 1
                e.wait_ge(sk, n)
                e.reciprocal(a_sk[:], a_sk[:]).then_inc(sk, 1)
                n += 1
                e.wait_ge(sk, n)
                e.tensor_tensor(t3, P3, a_sk[:].rearrange(
                    "p (s i) -> p s i", i=4).unsqueeze(3)
                    .broadcast_to([128, 16, 4, 4]), OP.mult).then_inc(sk, 1)
                n += 1
                e.wait_ge(sk, n)
                e.tensor_reduce(b2, t3T, AX.X, OP.add).then_inc(sk, 1)
                n += 1
                e.wait_ge(sk, n)
                e.reciprocal(b_sk[:], b_sk[:]).then_inc(sk, 1)
                n += 1
                e.wait_ge(sk, n)


def _bcast_sel(e, C, ps, src_fm, krows, r):
    last = None
    for tg in range(4):
        s = slice(tg * 512, (tg + 1) * 512)
        last = e.matmul(ps[0:128, s],
                        C["selb_sb"][0:krows, r * 128:(r + 1) * 128],
                        src_fm[0:krows, s], start=True, stop=True)
    return last


def _ffn_C(nc, t, C, tag, xpre, y_sb, P_sk, a_sk, b_sk, t_sk,
           hpre_tok, hpost_tok, ztok, h_fm):
    with (
        nc.sbuf_tensor(f"{tag}D_wbr", [128, 8 * 1024], BF16) as wbr_sb,
        nc.psum_tensor(f"{tag}D_p0", [128, T], F32) as py0,
        nc.psum_tensor(f"{tag}D_p1", [128, T], F32) as py1,
        perm_sem(nc, f"{tag}D_wl") as wl,
        perm_sem(nc, f"{tag}D_mm") as mm,
        perm_sem(nc, f"{tag}D_ev") as ev,
        perm_sem(nc, f"{tag}D_hb") as hb,
        nc.Block() as blk,
    ):
        pys = [py0, py1]

        @blk.gpsimd
        def _(e):
            e.dma_start(out=wbr_sb[:].rearrange("p (k j) -> p k j", j=1024),
                        in_=t["w_br"][:, :, :].rearrange("k p j -> p k j")
                        ).then_inc(wl, 16)

        @blk.tensor
        def _(e):
            e.wait_ge(wl, 16)
            for m8 in range(8):
                if m8 >= 2:
                    e.wait_ge(ev, m8 - 1)
                last = None
                for kc in range(8):
                    for tg in range(4):
                        s = slice(tg * 512, (tg + 1) * 512)
                        last = e.matmul(
                            pys[m8 % 2][:, s],
                            wbr_sb[:, kc * 1024 + m8 * 128:kc * 1024 + (m8 + 1) * 128],
                            xpre[:, kc * T + tg * 512:kc * T + (tg + 1) * 512],
                            start=(kc == 0), stop=(kc == 7))
                last.then_inc(mm, 1)
            e.wait_ge(ev, 8)
            e.wait_ge(hb, 6)
            _t_tok_to_rows(e, C, py0, ztok, 24).then_inc(mm, 1)

        @blk.scalar
        def _(e):
            for m8 in range(8):
                e.wait_ge(mm, m8 + 1)
                e.copy(y_sb[:, m8 * T:(m8 + 1) * T], pys[m8 % 2][:, :]).then_inc(ev, 1)

        @blk.vector
        def _(e):
            P3 = P_sk[:].rearrange("p (s i j) -> p s i j", i=4, j=4)
            t3 = t_sk[:].rearrange("p (s i j) -> p s i j", i=4, j=4)
            n = 0
            e.tensor_tensor(t3, P3, a_sk[:].rearrange("p (s i) -> p s i", i=4)
                            .unsqueeze(3).broadcast_to([128, 16, 4, 4]),
                            OP.mult).then_inc(hb, 1)
            n += 1
            e.wait_ge(hb, n)
            e.tensor_tensor(t3, t3, b_sk[:].rearrange("p (s j) -> p s j", j=4)
                            .unsqueeze(2).broadcast_to([128, 16, 4, 4]),
                            OP.mult).then_inc(hb, 1)
            n += 1
            e.wait_ge(hb, n)
            e.tensor_scalar(t_sk[:], t_sk[:], C["gcol_sb"][:], None, OP.mult
                            ).then_inc(hb, 1)
            n += 1
            e.wait_ge(hb, n)
            dg = t_sk[:].rearrange("p (s k) -> p s k", k=16)[:, :, 0:16:5]
            e.tensor_scalar(dg, dg, C["g1col_sb"][:], None, OP.add).then_inc(hb, 1)
            n += 1
            e.wait_ge(hb, n)
            zc = ztok[:].rearrange("p (s k) -> p s k", k=24)
            e.tensor_copy(zc[:, :, 0:16],
                          t_sk[:].rearrange("p (s k) -> p s k", k=16))
            e.tensor_scalar(hpost_tok[:], hpost_tok[:], 0.5, None, OP.mult
                            ).then_inc(hb, 1)
            n += 1
            e.wait_ge(hb, n)
            e.tensor_copy(zc[:, :, 16:20],
                          hpre_tok[:].rearrange("p (s k) -> p s k", k=4))
            e.tensor_copy(zc[:, :, 20:24],
                          hpost_tok[:].rearrange("p (s k) -> p s k", k=4)
                          ).then_inc(hb, 1)
            e.wait_ge(mm, 9)
            e.tensor_copy(h_fm[:], py0[0:24, :]).then_inc(hb, 1)


def _ffn_D(nc, t, C, xin, xout, tag, h_fm, y_sb):
    with (
        nc.sbuf_tensor(f"{tag}E_ht", [128, 10 * T], BF16) as htl,
        nc.sbuf_tensor(f"{tag}E_xl", [128, 4 * T], BF16) as xlb,
        nc.sbuf_tensor(f"{tag}E_x0", [128, 2 * T], F32) as x0b,
        nc.sbuf_tensor(f"{tag}E_ac", [128, 2 * T], BF16) as accb,
        nc.sbuf_tensor(f"{tag}E_tp", [128, T], BF16) as tmp,
        nc.sbuf_tensor(f"{tag}E_x1", [128, 2 * T], F32) as x1b,
        nc.psum_tensor(f"{tag}E_ps", [128, T], F32) as ps,
        perm_sem(nc, f"{tag}E_hb") as hbS,
        perm_sem(nc, f"{tag}E_he") as heS,
        perm_sem(nc, f"{tag}E_ld") as ld,
        perm_sem(nc, f"{tag}E_mx") as mx,
        perm_sem(nc, f"{tag}E_rs") as rs,
        perm_sem(nc, f"{tag}E_pz") as pz,
        perm_sem(nc, f"{tag}E_st") as stS,
        nc.Block() as blk,
    ):
        def rows_for(ipair):
            rws = []
            for i in (2 * ipair, 2 * ipair + 1):
                rws += [i * 4 + j for j in range(4)]
            rws += [20 + 2 * ipair, 20 + 2 * ipair + 1]
            return rws

        @blk.tensor
        def _(e):
            nb = 0
            for ipair in range(2):
                if ipair == 1:
                    e.wait_ge(mx, 16)
                for k, r in enumerate(rows_for(ipair)):
                    last = _bcast_sel(e, C, ps, h_fm, 24, r)
                    nb += 1
                    last.then_inc(hbS, 1)
                    e.wait_ge(heS, nb)

        @blk.scalar
        def _(e):
            nb = 0
            for ipair in range(2):
                for k in range(10):
                    nb += 1
                    e.wait_ge(hbS, nb)
                    e.copy(htl[:, k * T:(k + 1) * T], ps[0:128, :]).then_inc(heS, 1)

        @blk.sync
        def _(e):
            for ipair in range(2):
                for c8 in range(8):
                    it = ipair * 8 + c8
                    if it >= 1:
                        e.wait_ge(mx, 2 * it)
                    for j in range(4):
                        ch = j * 8 + c8
                        e.dma_start(out=xlb[:, j * T:(j + 1) * T],
                                    in_=t["xln_sp"][ch * 128:(ch + 1) * 128, :]
                                    ).then_inc(ld, 16)
                    for ii in range(2):
                        ch = (2 * ipair + ii) * 8 + c8
                        e.dma_start(out=x0b[:, ii * T:(ii + 1) * T],
                                    in_=xin[ch * 128:(ch + 1) * 128, :]
                                    ).then_inc(ld, 16)

        @blk.vector
        def _(e):
            vs = 0
            for ipair in range(2):
                for c8 in range(8):
                    it = ipair * 8 + c8
                    e.wait_ge(ld, (it + 1) * 96)
                    e.wait_ge(heS, (ipair + 1) * 10)
                    if it >= 1:
                        e.wait_ge(stS, 32 * it)
                    for ii in range(2):
                        acc = accb[:, ii * T:(ii + 1) * T]
                        e.tensor_mul(acc, xlb[:, 0:T],
                                     htl[:, (ii * 4) * T:(ii * 4 + 1) * T]
                                     ).then_inc(pz, 1)
                        vs += 1
                        for j in range(1, 4):
                            e.tensor_mul(tmp[:], xlb[:, j * T:(j + 1) * T],
                                         htl[:, (ii * 4 + j) * T:(ii * 4 + j + 1) * T]
                                         ).then_inc(pz, 1)
                            vs += 1
                            e.wait_ge(pz, vs)
                            e.tensor_add(acc, acc, tmp[:]).then_inc(pz, 1)
                            vs += 1
                            e.wait_ge(pz, vs)
                        e.tensor_mul(tmp[:], y_sb[:, c8 * T:(c8 + 1) * T],
                                     htl[:, (8 + ii) * T:(8 + ii + 1) * T]
                                     ).then_inc(pz, 1)
                        vs += 1
                        e.wait_ge(pz, vs)
                        e.tensor_add(acc, acc, tmp[:]).then_inc(pz, 1)
                        vs += 1
                        e.wait_ge(pz, vs)
                        e.tensor_add(x1b[:, ii * T:(ii + 1) * T],
                                     x0b[:, ii * T:(ii + 1) * T], acc
                                     ).then_inc(mx, 1)

        @blk.gpsimd
        def _(e):
            n = 0
            for ipair in range(2):
                for c8 in range(8):
                    it = ipair * 8 + c8
                    for ii in range(2):
                        e.wait_ge(mx, 2 * it + ii + 1)
                        ch = (2 * ipair + ii) * 8 + c8
                        e.dma_start(out=xout[ch * 128:(ch + 1) * 128, :],
                                    in_=x1b[:, ii * T:(ii + 1) * T]
                                    ).then_inc(stS, 16)
                        n += 16
            e.wait_ge(stS, n)



def phase_ln2(nc, t, C, xln2):
    with (
        nc.sbuf_tensor("l2_m", [1, T], F32) as m_row,
        nc.sbuf_tensor("l2_v", [1, T], F32) as v_row,
        nc.sbuf_tensor("l2_r", [1, T], F32) as r_row,
        nc.psum_tensor("l2_ps", [128, T], F32) as ps,
        nc.psum_tensor("l2_pm", [128, T], F32) as pm,
        nc.sbuf_tensor("l2_x0", [128, 2 * T], F32) as x0b,
        nc.sbuf_tensor("l2_xb", [128, T], BF16) as xbb,
        nc.sbuf_tensor("l2_sq", [128, T], BF16) as sqb,
    ):
        mr_row = v_row
        with (
            perm_sem(nc, "l2_ld") as ld,
            perm_sem(nc, "l2_cv") as cv,
            perm_sem(nc, "l2_sq") as sq,
            perm_sem(nc, "l2_mm") as mm,
            nc.Block() as blk,
        ):
            @blk.sync
            def _(e):
                for c in range(32):
                    if c >= 2:
                        e.wait_ge(cv, c - 1)
                        e.wait_ge(sq, c - 1)
                    e.dma_start(out=x0b[:, (c % 2) * T:(c % 2 + 1) * T],
                                in_=t["x1_sp"][c * 128:(c + 1) * 128, :]).then_inc(ld, 16)

            @blk.vector
            def _(e):
                for c in range(32):
                    e.wait_ge(ld, (c + 1) * 16)
                    if c >= 1:
                        e.wait_ge(mm, c)
                    sl = slice((c % 2) * T, (c % 2 + 1) * T)
                    e.tensor_copy(xbb[:], x0b[:, sl]).then_inc(cv, 1)

            @blk.scalar
            def _(e):
                for c in range(32):
                    e.wait_ge(ld, (c + 1) * 16)
                    if c >= 1:
                        e.wait_ge(mm, c)
                    sl = slice((c % 2) * T, (c % 2 + 1) * T)
                    e.activation(sqb[:], x0b[:, sl], AF.Square).then_inc(sq, 1)

            @blk.tensor
            def _(e):
                for c in range(32):
                    e.wait_ge(cv, c + 1)
                    e.wait_ge(sq, c + 1)
                    last = None
                    for tg in range(4):
                        s = slice(tg * 512, (tg + 1) * 512)
                        e.matmul(ps[0:1, s], C["ones_c_b"][:],
                                 xbb[:, tg * 512:(tg + 1) * 512],
                                 start=(c == 0), stop=(c == 31))
                        last = e.matmul(ps[32:33, s], C["ones_c_b"][:],
                                        sqb[:, tg * 512:(tg + 1) * 512],
                                        start=(c == 0), stop=(c == 31))
                    last.then_inc(mm, 1)

        ops = [
            ("vector", False, lambda e: e.tensor_scalar(m_row[:], ps[0:1, :], 1.0 / DIM, None, OP.mult)),
            ("vector", False, lambda e: e.tensor_scalar(v_row[:], ps[32:33, :], 1.0 / DIM, None, OP.mult)),
            ("vector", False, lambda e: e.tensor_mul(r_row[:], m_row[:], m_row[:])),
            ("vector", False, lambda e: e.tensor_sub(v_row[:], v_row[:], r_row[:])),
            ("vector", False, lambda e: e.tensor_scalar(v_row[:], v_row[:], LN_EPS, None, OP.add)),
            ("scalar", False, lambda e: e.activation(r_row[:], v_row[:], AF.Ln)),
            ("scalar", False, lambda e: e.activation(r_row[:], r_row[:], AF.Exp, scale=-0.5)),
            ("vector", False, lambda e: e.tensor_mul(mr_row[:], m_row[:], r_row[:])),
            ("tensor", False, lambda e: _bcast_full(e, C, ps, r_row)),
            ("tensor", False, lambda e: _bcast_full(e, C, pm, mr_row)),
        ]
        chain(nc, "l2p", ops)

        with (
            perm_sem(nc, "l2b_ld") as ld2,
            perm_sem(nc, "l2b_cv") as cv2,
            nc.Block() as blk2,
        ):
            @blk2.sync
            def _(e):
                for c in range(32):
                    if c >= 2:
                        e.wait_ge(cv2, 2 * (c - 1))
                    e.dma_start(out=x0b[:, (c % 2) * T:(c % 2 + 1) * T],
                                in_=t["x1_sp"][c * 128:(c + 1) * 128, :]).then_inc(ld2, 16)

            @blk2.vector
            def _(e):
                for c in range(32):
                    e.wait_ge(ld2, (c + 1) * 16)
                    sl = slice((c % 2) * T, (c % 2 + 1) * T)
                    e.tensor_mul(x0b[:, sl], x0b[:, sl], ps[0:128, :]).then_inc(cv2, 1)
                    e.wait_ge(cv2, 2 * c + 1)
                    e.tensor_sub(xln2[:, c * T:(c + 1) * T], x0b[:, sl],
                                 pm[0:128, :]).then_inc(cv2, 1)



def phase_inproj(nc, t, C, xln2, dtraw):
    order = HBC_BLKS + [DT_BLK] + GATE_BLKS
    spill_idx = {}
    si = 0
    for j, m in enumerate(order):
        if m == DT_BLK:
            spill_idx[j] = None
        else:
            spill_idx[j] = si
            si += 1
    n_spill = si
    stash_at = len(HBC_BLKS)

    with (
        nc.sbuf_tensor("ip_w", [128, 2 * 4096], BF16) as wbuf,
        nc.sbuf_tensor("ip_h", [128, 4 * T], BF16) as hbuf,
        nc.sbuf_tensor("ip_st", [128, 240], F32) as stash,
        nc.psum_tensor("ip_p0", [128, T], F32) as pp0,
        nc.psum_tensor("ip_p1", [128, T], F32) as pp1,
        perm_sem(nc, "ip_wl") as wl,
        perm_sem(nc, "ip_mm") as mm,
        perm_sem(nc, "ip_e0") as e0,
        perm_sem(nc, "ip_e1") as e1,
        perm_sem(nc, "ip_sp") as sp,
        perm_sem(nc, "ip_cc") as ccs,
        perm_sem(nc, "ip_vse") as vse,
        nc.Block() as blk,
    ):
        pps = [pp0, pp1]
        evs = [e0, e1]

        def spill_wait(j):
            jj = j - 4
            if jj < 0:
                return 0
            k = spill_idx[jj]
            if k is None:
                k = spill_idx[jj - 1]
                if k is None:
                    return 0
            n = 16 * (k + 1)
            if k + 1 >= stash_at:
                n += 16
            return n

        @blk.gpsimd
        def _(e):
            for j, m in enumerate(order):
                if j >= 2:
                    e.wait_ge(mm, j - 1)
                e.dma_start(out=wbuf[:, (j % 2) * 4096:(j % 2 + 1) * 4096],
                            in_=t["w_in"][m, :, :]).then_inc(wl, 16)
            e.wait_ge(sp, 16 * len(HBC_BLKS) + 16)
            e.collective_compute(
                "AllGather", OP.bypass, replica_groups=RG_PAIRS,
                ins=[t["cc1i"].opt()], outs=[t["cc1o"].opt()]).then_inc(ccs, 1)
            e.wait_ge(ccs, 1)

        @blk.tensor
        def _(e):
            for j, m in enumerate(order):
                e.wait_ge(wl, (j + 1) * 16)
                if j >= 2:
                    e.wait_ge(evs[j % 2], j // 2)
                last = None
                for kc in range(32):
                    lh = wbuf[:, (j % 2) * 4096 + kc * 128:(j % 2) * 4096 + (kc + 1) * 128]
                    for tg in range(4):
                        last = e.matmul(pps[j % 2][:, tg * 512:(tg + 1) * 512],
                                        lh, xln2[:, kc * T + tg * 512:kc * T + (tg + 1) * 512],
                                        start=(kc == 0), stop=(kc == 31))
                last.then_inc(mm, 1)

        @blk.vector
        def _(e):
            vv = 0
            for j, m in enumerate(order):
                if m in GATE_BLKS:
                    continue
                e.wait_ge(mm, j + 1)
                e.wait_ge(sp, spill_wait(j))
                if m == DT_BLK:
                    e.tensor_copy(dtraw[:], pps[j % 2][:, :]).then_inc(evs[j % 2], 1)
                else:
                    bi = m - 64
                    hb = hbuf[:, (j % 4) * T:(j % 4 + 1) * T]
                    e.tensor_copy(hb, pps[j % 2][:, :]).then_inc(vse, 1)
                    vv += 1
                    e.wait_ge(vse, vv)
                    e.tensor_copy(stash[:, bi * 3:(bi + 1) * 3],
                                  hb[:, 2045:2048]).then_inc(evs[j % 2], 1)

        @blk.scalar
        def _(e):
            for j, m in enumerate(order):
                if m not in GATE_BLKS:
                    continue
                e.wait_ge(mm, j + 1)
                e.wait_ge(sp, spill_wait(j))
                hb = hbuf[:, (j % 4) * T:(j % 4 + 1) * T]
                e.activation(hb, pps[j % 2][:, :], AF.Silu).then_inc(evs[j % 2], 1)

        @blk.sync
        def _(e):
            nsp = 0
            for j, m in enumerate(order):
                if m == DT_BLK:
                    continue
                e.wait_ge(evs[j % 2], j // 2 + 1)
                hb = hbuf[:, (j % 4) * T:(j % 4 + 1) * T]
                if m in GATE_BLKS:
                    dst = t["gpl_sp"][m * 128:(m + 1) * 128, :]
                else:
                    bi = m - 64
                    dst = t["hbc_sp"][bi * 128:(bi + 1) * 128, :]
                e.dma_start(out=dst, in_=hb).then_inc(sp, 16)
                nsp += 16
                if spill_idx[j] == stash_at - 1:
                    e.dma_start(out=t["cc1i"][:, :], in_=stash[:]).then_inc(sp, 16)
                    nsp += 16
            e.wait_ge(sp, nsp)





def phase_conv(nc, t, C):
    with (
        nc.sbuf_tensor("cv_tail", [128, 240], F32) as tailf,
        nc.sbuf_tensor("cv_ext", [128, 2 * 2051], BF16) as ext,
        nc.sbuf_tensor("cv_w", [128, 2 * 512], BF16) as cw,
        nc.sbuf_tensor("cv_o", [128, 4 * T], BF16) as cvo,
        nc.psum_tensor("cv_p0", [128, T], F32) as pp0,
        nc.psum_tensor("cv_p1", [128, T], F32) as pp1,
        perm_sem(nc, "cv_ld") as ld,
        perm_sem(nc, "cv_wl") as wl,
        perm_sem(nc, "cv_tl") as tl,
        perm_sem(nc, "cv_mm") as mm,
        perm_sem(nc, "cv_ev") as ev,
        perm_sem(nc, "cv_sp") as sp,
        nc.Block() as blk,
    ):
        pps = [pp0, pp1]

        @blk.sync
        def _(e):
            e.dma_start(out=tailf[:], in_=t["cc1o"][0:128, :]).then_inc(tl, 16)
            for bi in range(80):
                if bi >= 2:
                    e.wait_ge(mm, bi - 1)
                e.dma_start(out=ext[:, (bi % 2) * 2051 + 3:(bi % 2) * 2051 + 2051],
                            in_=t["hbc_sp"][bi * 128:(bi + 1) * 128, :]).then_inc(ld, 16)

        @blk.gpsimd
        def _(e):
            for bi in range(80):
                if bi >= 2:
                    e.wait_ge(mm, bi - 1)
                e.dma_start(out=cw[:, (bi % 2) * 512:(bi % 2 + 1) * 512]
                            .rearrange("p (k l) -> p k l", k=KCONV),
                            in_=t["convd"][bi, :, :, :].rearrange("k p l -> p k l")
                            ).then_inc(wl, 16)

        @blk.vector
        def _(e):
            e.wait_ge(tl, 16)
            e.tensor_scalar(tailf[:], tailf[:], C["oddm_sb"][:], None, OP.mult
                            ).then_inc(tl, 1)
            for bi in range(80):
                e.wait_ge(tl, 17)
                e.wait_ge(ld, (bi + 1) * 16)
                if bi >= 2:
                    e.wait_ge(mm, bi - 1)
                e.tensor_copy(ext[:, (bi % 2) * 2051:(bi % 2) * 2051 + 3],
                              tailf[:, bi * 3:(bi + 1) * 3]).then_inc(ld, 1)

        @blk.tensor
        def _(e):
            for bi in range(80):
                e.wait_ge(wl, (bi + 1) * 16)
                e.wait_ge(ld, (bi + 1) * 16 + (bi + 1))
                if bi >= 2:
                    e.wait_ge(ev, bi - 1)
                o = (bi % 2) * 2051
                last = None
                for k in range(KCONV):
                    lh = cw[:, (bi % 2) * 512 + k * 128:(bi % 2) * 512 + (k + 1) * 128]
                    for tg in range(4):
                        last = e.matmul(pps[bi % 2][:, tg * 512:(tg + 1) * 512],
                                        lh, ext[:, o + k + tg * 512:o + k + (tg + 1) * 512],
                                        start=(k == 0), stop=(k == KCONV - 1))
                last.then_inc(mm, 1)

        @blk.scalar
        def _(e):
            for bi in range(80):
                e.wait_ge(mm, bi + 1)
                if bi >= 4:
                    e.wait_ge(sp, 16 * (bi - 3))
                e.activation(cvo[:, (bi % 4) * T:(bi % 4 + 1) * T],
                             pps[bi % 2][:, :], AF.Silu,
                             bias=C["convb_sb"][:, bi:bi + 1]).then_inc(ev, 1)
                e.wait_ge(ev, bi + 1)
                if bi < 64:
                    dst = t["xpl_sp"][bi * 128:(bi + 1) * 128, :]
                elif bi < 72:
                    dst = t["bpl_sp"][(bi - 64) * 128:(bi - 63) * 128, :]
                else:
                    dst = t["cpl_sp"][(bi - 72) * 128:(bi - 71) * 128, :]
                e.dma_start(out=dst, in_=cvo[:, (bi % 4) * T:(bi % 4 + 1) * T]
                            ).then_inc(sp, 16)
            e.wait_ge(sp, 16 * 80)



def phase_ssd_prep(nc, t, C, dtraw, S):
    cum_fm, dt_tok, cum_tok = S["cum_fm"], S["dt_tok"], S["cum_tok"]
    csrows, ecsrows = S["csrows"], S["ecsrows"]
    with (
        nc.sbuf_tensor("pr_z", [128, T], F32) as z,
        nc.sbuf_tensor("pr_a", [128, T], F32) as az,
        nc.sbuf_tensor("pr_cst", [16, 128], F32) as cst,
        nc.psum_tensor("pr_p", [128, T], F32) as pp,
    ):
        def scans(e):
            last = None
            for c in range(NCH):
                s = slice(c * 128, (c + 1) * 128)
                last = e.tensor_tensor_scan(cum_fm[:, s], C["ones_t_f"][:],
                                            z[:, s], 0.0, OP.mult, OP.add)
            return last

        def trans_all(e, src):
            last = None
            for c in range(NCH):
                s = slice(c * 128, (c + 1) * 128)
                last = e.transpose(pp[0:128, s], src[:, s], C["ident_f"][:])
            return last

        ops = [
            ("vector", False, lambda e: e.tensor_scalar(z[:], dtraw[:], C["dtb_sb"][:], None, OP.add)),
            ("scalar", False, lambda e: e.activation(az[:], z[:], AF.Abs)),
            ("scalar", False, lambda e: e.activation(az[:], az[:], AF.Exp, scale=-1.0)),
            ("scalar", False, lambda e: e.activation(az[:], az[:], AF.Ln, bias=1.0)),
            ("vector", False, lambda e: e.tensor_scalar(dtraw[:], z[:], 0.0, None, OP.max)),
            ("vector", False, lambda e: e.tensor_add(dtraw[:], dtraw[:], az[:])),
            ("vector", False, lambda e: e.tensor_scalar(z[:], dtraw[:], C["nega_sb"][:], None, OP.mult)),
            ("vector", False, scans),
            ("tensor", False, lambda e: trans_all(e, dtraw)),
            ("vector", False, lambda e: e.tensor_copy(dt_tok[:], pp[0:128, :])),
            ("tensor", False, lambda e: trans_all(e, cum_fm)),
            ("vector", False, lambda e: e.tensor_copy(cum_tok[:], pp[0:128, :])),
            ("tensor", False, lambda e: e.transpose(
                pp[0:16, 0:128],
                cum_fm[:].rearrange("p (c l) -> p c l", l=128)[:, :, 127],
                C["ident_f"][:])),
            ("vector", False, lambda e: e.tensor_copy(cst[:], pp[0:16, 0:128])),
            ("sync", True, lambda e: e.dma_start(out=t["csr_sp"][:, :], in_=cst[:])),
            ("sync", True, lambda e: e.dma_start(
                out=csrows[:].rearrange("o (c l) -> o c l", l=128),
                in_=t["csr_sp"][:, :])),
            ("scalar", False, lambda e: e.activation(ecsrows[:], csrows[:], AF.Exp)),
        ]
        chain(nc, "ssdprep", ops)



class Sched:

    def __init__(self):
        self.ops = []
        self.cnt = {}
        self.pos = []

    def add(self, engine, fn, deps=(), dma=False):
        inc = 16 if dma else 1
        track = "gpsimd@dma" if (dma and engine == "gpsimd") else engine
        cum = self.cnt.get(track, 0) + inc
        self.cnt[track] = cum
        self.ops.append((engine, fn,
                         tuple(d for d in deps if d is not None), inc, track))
        self.pos.append((track, cum, engine))
        return len(self.ops) - 1

    def emit(self, nc, name, limit=None):
        import contextlib
        if limit is not None:
            self.ops = self.ops[:limit]
        engines = sorted(set(en for en, _, _, _, _ in self.ops))
        tracks = sorted(set(tr for _, _, _, _, tr in self.ops))
        with nc.Block() as blk:
            sems = {}
            with contextlib.ExitStack() as st:
                for tr in tracks:
                    sems[tr] = st.enter_context(perm_sem(nc, f"{name}_{tr}"))

                for en in engines:
                    mine = [(i, fn, deps, inc, tr)
                            for i, (e2, fn, deps, inc, tr)
                            in enumerate(self.ops) if e2 == en]

                    def body(e, mine=mine):
                        for i, fn, deps, inc, tr in mine:
                            waits = {}
                            for d in deps:
                                dtr, dcum, den = self.pos[d]
                                if waits.get(dtr, 0) < dcum:
                                    waits[dtr] = dcum
                            for dtr, v in sorted(waits.items()):
                                e.wait_ge(sems[dtr], v)
                            fn(e).then_inc(sems[tr], inc)
                    getattr(blk, en)(body)


_P1_CHUNKS = NCH
_P1_COLL = True
_P1_LIMIT = None
_P2_LIMIT = None


def phase_ssd_pass1(nc, t, C, S):
    import contextlib
    state = S["state"]
    NB = 16
    with contextlib.ExitStack() as es:
        xdt = es.enter_context(nc.sbuf_tensor("p1_xd", [128, 2 * INTER], BF16))
        btok = es.enter_context(nc.sbuf_tensor("p1_bt", [128, 2 * 1024], BF16))
        bdec = es.enter_context(nc.sbuf_tensor("p1_bd", [128, 2 * 2048], BF16))
        wdec = es.enter_context(nc.sbuf_tensor("p1_wd", [128, 128], BF16))
        wtmp = es.enter_context(nc.sbuf_tensor("p1_wt", [128, 128], F32))
        wsc = es.enter_context(nc.sbuf_tensor("p1_ws", [128, 128], F32))
        sstage = es.enter_context(nc.sbuf_tensor("p1_ss", [128, INTER], BF16))
        pw = es.enter_context(nc.psum_tensor("p1_pw", [128, 256], F32))
        ps0 = es.enter_context(nc.psum_tensor("p1_s0", [128, 512], F32))
        ps1 = es.enter_context(nc.psum_tensor("p1_s1", [128, 512], F32))
        pss = [ps0, ps1]
        sc = Sched()
        prev = {}
        for c in range(_P1_CHUNKS):
            cm = c % 2
            s1 = sc.add("sync", lambda e, c=c, cm=cm: e.dma_start(
                out=xdt[:, cm * INTER:(cm + 1) * INTER],
                in_=t["xpl_sp"][:, c * 128:(c + 1) * 128], transpose=True),
                deps=[prev.get(("smm_last", c - 2))], dma=True)
            s2 = sc.add("sync", lambda e, c=c, cm=cm: e.dma_start(
                out=btok[:, cm * 1024:(cm + 1) * 1024],
                in_=t["bpl_sp"][:, c * 128:(c + 1) * 128], transpose=True),
                deps=[prev.get(("bdec_last", c - 2))], dma=True)
            wb = sc.add("tensor", lambda e, c=c: _p1_wb(e, C, S, pw, c),
                        deps=[prev.get(("wsub", c - 1)), prev.get(("wcp", c - 1))])
            vxd = sc.add("vector", lambda e, c=c, cm=cm: e.tensor_tensor(
                xdt[:, cm * INTER:(cm + 1) * INTER].rearrange("p (h q) -> p h q", q=64),
                xdt[:, cm * INTER:(cm + 1) * INTER].rearrange("p (h q) -> p h q", q=64),
                dt_view(S, c), OP.mult), deps=[s1])
            wsub = sc.add("vector", lambda e, c=c: e.tensor_sub(
                wtmp[:], pw[:, 0:128], S["cum_tok"][:, c * 128:(c + 1) * 128]),
                deps=[wb])
            wcp = sc.add("vector", lambda e: e.tensor_copy(wsc[:], pw[:, 128:256]),
                         deps=[wb, prev.get(("pmul", c - 1))])
            wexp = sc.add("scalar", lambda e: e.activation(wdec[:], wtmp[:], AF.Exp),
                          deps=[wsub, prev.get(("bdec_last", c - 1))])
            if c >= 1:
                pmul = sc.add("gpsimd", lambda e: e.tensor_tensor(
                    state[:].rearrange("p (h q) -> p h q", q=64),
                    state[:].rearrange("p (h q) -> p h q", q=64),
                    wsc[:].unsqueeze(2).broadcast_to([128, 128, 64]), OP.mult),
                    deps=[wcp, prev.get(("vadd_last", c - 1))])
            else:
                pmul = None
            prev[("pmul", c)] = pmul
            bdec_ids = []
            for g in range(G):
                bd = sc.add("vector", lambda e, g=g, cm=cm: e.tensor_tensor(
                    bdec[:, (g % 2) * 2048:(g % 2) * 2048 + 2048]
                    .rearrange("p (h n) -> p h n", h=16),
                    btok[:, cm * 1024 + g * 128:cm * 1024 + (g + 1) * 128]
                    .unsqueeze(1).broadcast_to([128, 16, 128]),
                    wdec[:, g * 16:(g + 1) * 16].unsqueeze(2)
                    .broadcast_to([128, 16, 128]), OP.mult),
                    deps=[wexp, s2,
                          prev.get(("smm", c, 2 * (g - 2) + 1)) if g >= 2 else
                          prev.get(("smm", c - 1, 2 * (g + 8 - 2) + 1))])
                bdec_ids.append(bd)
                prev[("bdec", c, g)] = bd
                for bank in (2 * g, 2 * g + 1):
                    dep_free = prev.get(("sev", c, bank - 2)) if bank >= 2 else \
                        prev.get(("sev", c - 1, bank + 14))
                    dep_free2 = prev.get(("vadd", c, bank - 2)) if bank >= 2 else \
                        prev.get(("vadd", c - 1, bank + 14))
                    mmid = sc.add("tensor", lambda e, c=c, cm=cm, bank=bank, g=g:
                                  _p1_smm(e, pss, bdec, xdt, c, cm, bank, g),
                                  deps=[vxd, bd, dep_free, dep_free2])
                    prev[("smm", c, bank)] = mmid
                    ev = sc.add("scalar", lambda e, bank=bank: e.copy(
                        sstage[:, bank * 512:(bank + 1) * 512], pss[bank % 2][:, :]),
                        deps=[mmid, prev.get(("spill", c - 1))])
                    prev[("sev", c, bank)] = ev
                    if c == 0:
                        va = sc.add("vector", lambda e, bank=bank: e.tensor_copy(
                            state[:, bank * 512:(bank + 1) * 512],
                            sstage[:, bank * 512:(bank + 1) * 512]), deps=[ev])
                    else:
                        va = sc.add("vector", lambda e, bank=bank: e.tensor_add(
                            state[:, bank * 512:(bank + 1) * 512],
                            state[:, bank * 512:(bank + 1) * 512],
                            sstage[:, bank * 512:(bank + 1) * 512]),
                            deps=[ev, pmul])
                    prev[("vadd", c, bank)] = va
            prev[("bdec_last", c)] = bdec_ids[-1]
            prev[("smm_last", c)] = prev[("smm", c, NB - 1)]
            prev[("vadd_last", c)] = prev[("vadd", c, NB - 1)]
            if c == 0 and _tap(t, "p1xdt") is not None:
                sc.add("sync", lambda e: e.dma_start(
                    out=_tap(t, "p1xdt")[:, :], in_=xdt[:, 0:INTER]),
                    deps=[vxd, prev[("smm_last", 0)]], dma=True)
            if c == 0 and _tap(t, "p1bd") is not None:
                sc.add("sync", lambda e: e.dma_start(
                    out=_tap(t, "p1bd")[:, :], in_=bdec[:, 0:2048]),
                    deps=[bdec_ids[0], prev[("smm", 0, 1)]], dma=True)
            if c == 0 and _tap(t, "p1wd") is not None:
                sc.add("sync", lambda e: e.dma_start(
                    out=_tap(t, "p1wd")[:, :], in_=wdec[:]),
                    deps=[wexp, bdec_ids[-1]], dma=True)
            prev[("wsub", c)] = wsub
            prev[("wcp", c)] = wcp
            sp = sc.add("gpsimd", lambda e, c=c: e.dma_start(
                out=t["s_sp"][c, :, :], in_=sstage[:]),
                deps=[prev[("sev", c, NB - 1)]], dma=True)
            prev[("spill", c)] = sp
        if _P1_COLL:
            st1 = sc.add("sync", lambda e: e.dma_start(
                out=t["cc2i"][:, :], in_=state[:]),
                deps=[prev[("vadd_last", _P1_CHUNKS - 1)]], dma=True)
            cc = sc.add("gpsimd", lambda e: e.collective_compute(
                "AllGather", OP.bypass, replica_groups=RG_PAIRS,
                ins=[t["cc2i"].opt()], outs=[t["cc2o"].opt()]), deps=[st1])
            st2 = sc.add("sync", lambda e: e.dma_start(
                out=state[:], in_=t["cc2o"][0:128, :]), deps=[cc], dma=True)
            fin = sc.add("vector", lambda e: e.tensor_scalar(
                state[:], state[:], C["oddm_sb"][:], None, OP.mult), deps=[st2])
        sc.emit(nc, "p1", limit=_P1_LIMIT)


def _p1_wb(e, C, S, pw, c):
    e.matmul(pw[:, 0:128], C["ones_r_f"][:],
             S["csrows"][0:1, c * 128:(c + 1) * 128], start=True, stop=True)
    return e.matmul(pw[:, 128:256], C["ones_r_f"][:],
                    S["ecsrows"][0:1, c * 128:(c + 1) * 128], start=True, stop=True)


def _p1_smm(e, pss, bdec, xdt, c, cm, bank, g):
    last = None
    for hr8 in range(8):
        h = bank * 8 + hr8
        hr = h % 16
        last = e.matmul(
            pss[bank % 2][:, hr8 * 64:(hr8 + 1) * 64],
            bdec[:, (g % 2) * 2048 + hr * 128:(g % 2) * 2048 + (hr + 1) * 128],
            xdt[:, cm * INTER + h * 64:cm * INTER + (h + 1) * 64],
            start=True, stop=True)
    return last


def dt_view(S, c):
    return S["dt_tok"][:, c * 128:(c + 1) * 128].unsqueeze(2).broadcast_to(
        [128, 128, 64])


def phase_ssd_pass2(nc, t, C, S):
    import contextlib
    state = S["state"]
    with contextlib.ExitStack() as es:
        def sb(name, shape, dt):
            return es.enter_context(nc.sbuf_tensor(name, shape, dt))
        xdt = sb("p2_xd", [128, INTER], BF16)
        big = sb("p2_big", [128, INTER], BF16)
        gfm = sb("p2_gf", [128, INTER], BF16)
        bfm = sb("p2_bf", [128, 1024], BF16)
        cfm = sb("p2_cf", [128, 1024], BF16)
        state_bf = sb("p2_sb", [128, INTER], BF16)
        m_sb = sb("p2_m", [128, 1024], BF16)
        cdec = sb("p2_cd", [128, 1024], BF16)
        d_sb = sb("p2_d", [128, 1024], F32)
        l_sb = sb("p2_l", [128, 1024], BF16)
        expb = sb("p2_eb", [128, 1024], BF16)
        y_t = sb("p2_y", [128, INTER], BF16)
        ysq = sb("p2_ysq", [128, 2 * 512], BF16)
        dxs = sb("p2_dx", [128, 512], BF16)
        vrow = sb("p2_vr", [1, 128], F32)
        wsc2 = sb("p2_ws", [128, 128], F32)
        cflat = sb("p2_fl", [1, 4096], F32)
        pg = es.enter_context(nc.psum_tensor("p2_pg", [128, 1024], F32))
        pd0 = es.enter_context(nc.psum_tensor("p2_d0", [128, 1024], F32))
        pd1 = es.enter_context(nc.psum_tensor("p2_d1", [128, 1024], F32))
        py = es.enter_context(nc.psum_tensor("p2_py", [128, 512], F32))
        pmisc = es.enter_context(nc.psum_tensor("p2_ms", [128, 512], F32))
        pds = [pd0, pd1]
        sc = Sched()
        prev = {}
        for c in range(NCH):
            s1 = sc.add("sync", lambda e, c=c: e.dma_start(
                out=xdt[:], in_=t["xpl_sp"][:, c * 128:(c + 1) * 128],
                transpose=True), deps=[prev.get(("ymm", c - 1, 15))], dma=True)
            s2 = sc.add("sync", lambda e, c=c: e.dma_start(
                out=bfm[:].rearrange("n (g l) -> n g l", g=8),
                in_=t["bpl_sp"].rearrange("(g n) t -> n g t", n=128)
                [:, :, c * 128:(c + 1) * 128]),
                deps=[prev.get(("gmm", c - 1))], dma=True)
            s3 = sc.add("sync", lambda e, c=c: e.dma_start(
                out=cfm[:].rearrange("n (g l) -> n g l", g=8),
                in_=t["cpl_sp"].rearrange("(g n) t -> n g t", n=128)
                [:, :, c * 128:(c + 1) * 128]),
                deps=[prev.get(("cdec", c - 1, 15))], dma=True)
            s4 = sc.add("sync", lambda e, c=c: e.dma_start(
                out=gfm[:].rearrange("p (cc l) -> p cc l", cc=64),
                in_=t["gpl_sp"].rearrange("(cc p) t -> p cc t", p=128)
                [:, :, c * 128:(c + 1) * 128]),
                deps=[prev.get(("ygate", c - 1))], dma=True)
            if c >= 1:
                s5 = sc.add("sync", lambda e, c=c: e.dma_start(
                    out=big[:], in_=t["s_sp"][c - 1, :, :]),
                    deps=[prev.get(("dx", c - 1, 15))], dma=True)
            else:
                s5 = None
            if c >= 1:
                wscb = sc.add("tensor", lambda e, c=c: e.matmul(
                    pmisc[:, 0:128], C["ones_r_f"][:],
                    S["ecsrows"][0:1, (c - 1) * 128:c * 128], start=True, stop=True),
                    deps=[prev.get(("yr", c - 1))])
                wcp2 = sc.add("vector", lambda e: e.tensor_copy(
                    wsc2[:], pmisc[:, 0:128]), deps=[wscb])
                repm = sc.add("gpsimd", lambda e: e.tensor_tensor(
                    state[:].rearrange("p (h q) -> p h q", q=64),
                    state[:].rearrange("p (h q) -> p h q", q=64),
                    wsc2[:].unsqueeze(2).broadcast_to([128, 128, 64]),
                    OP.mult), deps=[wcp2, prev.get(("sbf", c - 1))])
                repa = sc.add("vector", lambda e: e.tensor_add(
                    state[:], state[:], big[:]), deps=[repm, s5])
            else:
                wscb = repm = repa = None
            sbf = sc.add("vector", lambda e: e.tensor_copy(state_bf[:], state[:]),
                         deps=[repa])
            prev[("sbf", c)] = sbf
            s6 = sc.add("sync", lambda e, c=c: e.dma_start(
                out=big[:].rearrange("p (cc l) -> p cc l", cc=64),
                in_=t["xpl_sp"].rearrange("(cc p) t -> p cc t", p=128)
                [:, :, c * 128:(c + 1) * 128]), deps=[repa, s5], dma=True)
            vxd = sc.add("vector", lambda e, c=c: e.tensor_tensor(
                xdt[:].rearrange("p (h q) -> p h q", q=64),
                xdt[:].rearrange("p (h q) -> p h q", q=64),
                dt_view(S, c), OP.mult), deps=[s1])
            gmm = sc.add("tensor", lambda e: _p2_gmm(e, pg, bfm, cfm),
                         deps=[s2, s3, prev.get(("mmul", c - 1, 15))])
            prev[("gmm", c)] = gmm
            fl = {}
            for sg in range(16):
                if sg % 4 == 0:
                    qq = sg // 4
                    f1 = sc.add("sync", lambda e, c=c, qq=qq: e.dma_start(
                        out=t["flt_sp"][0:32, :],
                        in_=S["cum_fm"][qq * 32:(qq + 1) * 32,
                                        c * 128:(c + 1) * 128]),
                        deps=[prev.get(("flb", c, qq - 1)) if qq else
                              prev.get(("flb", c - 1, 3))], dma=True)
                    f2 = sc.add("sync", lambda e: e.dma_start(
                        out=cflat[:].rearrange("o (h l) -> o h l", l=128),
                        in_=t["flt_sp"][0:32, :]),
                        deps=[f1, prev.get(("dmm", c, 4 * qq - 1)) if qq else
                              prev.get(("dmm", c - 1, 15))], dma=True)
                    fl[qq] = f2
                    prev[("flb", c, qq)] = f2
                dmm = sc.add("tensor", lambda e, sg=sg: _p2_dmm(
                    e, C, pds[sg % 2], cflat, sg),
                    deps=[fl[sg // 4], prev.get(("expb", c, sg - 2)),
                          prev.get(("dsub", c, sg - 2))])
                prev[("dmm", c, sg)] = dmm
                eb = sc.add("scalar", lambda e, sg=sg: e.activation(
                    expb[:], pds[sg % 2][:, :], AF.Exp),
                    deps=[dmm, prev.get(("cdec", c, sg - 1))])
                prev[("expb", c, sg)] = eb
                dsub = sc.add("vector", lambda e, c=c, sg=sg: e.tensor_tensor(
                    d_sb[:].rearrange("p (h l) -> p h l", h=8),
                    pds[sg % 2][:, :].rearrange("p (h l) -> p h l", h=8),
                    S["cum_tok"][:, c * 128 + sg * 8:c * 128 + sg * 8 + 8]
                    .unsqueeze(2).broadcast_to([128, 8, 128]), OP.subtract),
                    deps=[dmm, eb, prev.get(("tri", c, sg - 1)),
                          prev.get(("lexp", c, sg - 1))])
                prev[("dsub", c, sg)] = dsub
                tri = sc.add("gpsimd", lambda e: e.tensor_tensor(
                    d_sb[:].rearrange("p (h l) -> p h l", h=8),
                    d_sb[:].rearrange("p (h l) -> p h l", h=8),
                    C["tri_f"][:].unsqueeze(1).broadcast_to([128, 8, 128]), OP.add),
                    deps=[dsub, prev.get(("lexp", c, sg - 1))])
                prev[("tri", c, sg)] = tri
                lexp = sc.add("scalar", lambda e: e.activation(
                    l_sb[:], d_sb[:], AF.Exp),
                    deps=[tri, prev.get(("mmul", c, sg - 1))])
                prev[("lexp", c, sg)] = lexp
                mmul = sc.add("vector", lambda e, sg=sg: e.tensor_tensor(
                    m_sb[:].rearrange("p (h l) -> p h l", h=8),
                    pg[:, (sg // 2) * 128:(sg // 2 + 1) * 128].unsqueeze(1)
                    .broadcast_to([128, 8, 128]),
                    l_sb[:].rearrange("p (h l) -> p h l", h=8), OP.mult),
                    deps=[gmm, lexp, prev.get(("ymm", c, sg - 1))])
                prev[("mmul", c, sg)] = mmul
                cdm = sc.add("vector", lambda e, sg=sg: e.tensor_tensor(
                    cdec[:].rearrange("p (h l) -> p h l", h=8),
                    cfm[:, (sg // 2) * 128:(sg // 2 + 1) * 128].unsqueeze(1)
                    .broadcast_to([128, 8, 128]),
                    expb[:].rearrange("p (h l) -> p h l", h=8), OP.mult),
                    deps=[s3, eb, prev.get(("ymm", c, sg - 1))])
                prev[("cdec", c, sg)] = cdm
                ymm = sc.add("tensor", lambda e, sg=sg: _p2_ymm(
                    e, py, xdt, state_bf, m_sb, cdec, sg),
                    deps=[vxd, sbf, mmul, cdm, prev.get(("yev", c, sg - 1))])
                prev[("ymm", c, sg)] = ymm
                dx = sc.add("gpsimd", lambda e, sg=sg: e.tensor_tensor(
                    dxs[:].rearrange("p (a l) -> p a l", a=4),
                    big[:].rearrange("p (cc l) -> p cc l", cc=64)
                    [:, sg * 4:(sg + 1) * 4, :],
                    C["dcol_sb"][:, sg * 4:(sg + 1) * 4].unsqueeze(2)
                    .broadcast_to([128, 4, 128]), OP.mult),
                    deps=[s6, prev.get(("yev", c, sg - 1))])
                prev[("dx", c, sg)] = dx
                yev = sc.add("vector", lambda e, sg=sg: e.tensor_add(
                    y_t[:, sg * 512:(sg + 1) * 512], dxs[:], py[:, :]),
                    deps=[ymm, dx, prev.get(("ystore", c - 1)) if sg == 0 else None])
                prev[("yev", c, sg)] = yev
            sq_ids = []
            for i in range(16):
                sq = sc.add("scalar", lambda e, i=i: e.activation(
                    ysq[:, (i % 2) * 512:(i % 2 + 1) * 512],
                    y_t[:, i * 512:(i + 1) * 512], AF.Square),
                    deps=[prev[("yev", c, i)], prev.get(("stat", c, i - 2))])
                stat = sc.add("tensor", lambda e, c=c, i=i: e.matmul(
                    pmisc[0:1, 0:512], C["ones_c_b"][:],
                    ysq[:, (i % 2) * 512:(i % 2 + 1) * 512],
                    start=(i == 0), stop=(i == 15)),
                    deps=[sq, (prev.get(("repm2", c)) if False else None),
                          prev.get(("yr", c - 1)) if i == 0 else None,
                          repm if i == 0 else None])
                prev[("stat", c, i)] = stat
                sq_ids.append(sq)
            vred = sc.add("vector", lambda e: e.tensor_reduce(
                vrow[:], pmisc[0:1, 0:512].rearrange("o (a b) -> o b a", a=4),
                AX.X, OP.add), deps=[prev[("stat", c, 15)],
                                     prev.get(("rrb", c - 1))])
            veps = sc.add("vector", lambda e: e.tensor_scalar(
                vrow[:], vrow[:], 1.0 / INTER, LN_EPS, OP.mult, OP.add),
                deps=[vred])
            rr1 = sc.add("scalar", lambda e: e.activation(
                vrow[:], vrow[:], AF.Ln), deps=[veps])
            rr2 = sc.add("scalar", lambda e: e.activation(
                vrow[:], vrow[:], AF.Exp, scale=-0.5), deps=[rr1])
            rrb = sc.add("tensor", lambda e: e.matmul(
                pmisc[:, 0:128], C["ones_r_f"][:], vrow[0:1, 0:128],
                start=True, stop=True), deps=[rr2, vred])
            prev[("rrb", c)] = rrb
            ygate = sc.add("vector", lambda e: e.tensor_tensor(
                y_t[:].rearrange("p (cc l) -> p cc l", cc=64),
                y_t[:].rearrange("p (cc l) -> p cc l", cc=64),
                gfm[:].rearrange("p (cc l) -> p cc l", cc=64), OP.mult),
                deps=[s4, prev[("yev", c, 15)], sq_ids[15]])
            prev[("ygate", c)] = ygate
            nwm = sc.add("gpsimd", lambda e: e.tensor_tensor(
                y_t[:].rearrange("p (cc l) -> p cc l", cc=64),
                y_t[:].rearrange("p (cc l) -> p cc l", cc=64),
                C["nrmw_sb"][:].unsqueeze(2).broadcast_to([128, 64, 128]),
                OP.mult), deps=[ygate])
            yr = sc.add("vector", lambda e: e.tensor_tensor(
                y_t[:].rearrange("p (cc l) -> p cc l", cc=64),
                y_t[:].rearrange("p (cc l) -> p cc l", cc=64),
                pmisc[:, 0:128].unsqueeze(1).broadcast_to([128, 64, 128]),
                OP.mult), deps=[nwm, rrb])
            prev[("yr", c)] = yr
            yst = sc.add("sync", lambda e, c=c: e.dma_start(
                out=t["y_sp"].rearrange("cc p l -> p cc l")
                [:, :, c * 128:(c + 1) * 128],
                in_=y_t[:].rearrange("p (cc l) -> p cc l", cc=64)),
                deps=[yr], dma=True)
            prev[("ystore", c)] = yst
        sc.emit(nc, "p2", limit=_P2_LIMIT)


def _p2_gmm(e, pg, bfm, cfm):
    last = None
    for g in range(G):
        last = e.matmul(pg[:, g * 128:(g + 1) * 128],
                        bfm[:, g * 128:(g + 1) * 128],
                        cfm[:, g * 128:(g + 1) * 128], start=True, stop=True)
    return last


def _p2_dmm(e, C, pd, cflat, sg):
    base = (sg % 4) * 1024
    e.matmul(pd[:, 0:512], C["ones_r_f"][:], cflat[0:1, base:base + 512],
             start=True, stop=True)
    return e.matmul(pd[:, 512:1024], C["ones_r_f"][:],
                    cflat[0:1, base + 512:base + 1024], start=True, stop=True)


def _p2_ymm(e, py, xdt, state_bf, m_sb, cdec, sg):
    last = None
    for hr in range(8):
        h = sg * 8 + hr
        ro = (h % 2) * 64
        co = ((h % 8) // 2) * 128
        e.matmul(py[ro:ro + 64, co:co + 128], xdt[:, h * 64:(h + 1) * 64],
                 m_sb[:, hr * 128:(hr + 1) * 128], start=True, stop=False)
        last = e.matmul(py[ro:ro + 64, co:co + 128],
                        state_bf[:, h * 64:(h + 1) * 64],
                        cdec[:, hr * 128:(hr + 1) * 128], start=False, stop=True)
    return last


def phase_outproj(nc, t, C):
    with (
        nc.sbuf_tensor("op_y", [128, 64 * 1024], BF16) as yh,
        nc.sbuf_tensor("op_w", [128, 2 * 8192], BF16) as wbuf,
        nc.sbuf_tensor("op_x1", [128, 2 * 1024], F32) as x1b,
        nc.sbuf_tensor("op_x2", [128, 2 * 1024], F32) as x2b,
        nc.psum_tensor("op_p0", [128, 1024], F32) as pp0,
        nc.psum_tensor("op_p1", [128, 1024], F32) as pp1,
        perm_sem(nc, "op_yl") as yl,
        perm_sem(nc, "op_wl") as wl,
        perm_sem(nc, "op_xl") as xl,
        perm_sem(nc, "op_mm") as mm,
        perm_sem(nc, "op_ev") as ev,
        perm_sem(nc, "op_st") as stS,
        nc.Block() as blk,
    ):
        pps = [pp0, pp1]

        @blk.sync
        def _(e):
            nyl = 0
            for half in range(2):
                if half == 1:
                    e.wait_ge(mm, 32)
                for cc in range(64):
                    e.dma_start(out=yh[:, cc * 1024:(cc + 1) * 1024],
                                in_=t["y_sp"][cc, :, half * 1024:(half + 1) * 1024]
                                ).then_inc(yl, 16)
                    nyl += 16
                for mo in range(32):
                    it = half * 32 + mo
                    if it >= 2:
                        e.wait_ge(ev, it - 1)
                    e.dma_start(out=x1b[:, (it % 2) * 1024:(it % 2 + 1) * 1024],
                                in_=t["x1_sp"][mo * 128:(mo + 1) * 128,
                                               half * 1024:(half + 1) * 1024]
                                ).then_inc(xl, 16)

        @blk.gpsimd
        def _(e):
            for half in range(2):
                for mo in range(32):
                    it = half * 32 + mo
                    if it >= 2:
                        e.wait_ge(mm, it - 1)
                    e.dma_start(out=wbuf[:, (it % 2) * 8192:(it % 2 + 1) * 8192],
                                in_=t["w_out"][mo, :, :]).then_inc(wl, 16)

        @blk.tensor
        def _(e):
            for half in range(2):
                e.wait_ge(yl, (half + 1) * 64 * 16)
                for mo in range(32):
                    it = half * 32 + mo
                    e.wait_ge(wl, (it + 1) * 16)
                    if it >= 2:
                        e.wait_ge(ev, it - 1)
                    last = None
                    for kc in range(64):
                        lh = wbuf[:, (it % 2) * 8192 + kc * 128:(it % 2) * 8192 + (kc + 1) * 128]
                        for tg in range(2):
                            last = e.matmul(
                                pps[it % 2][:, tg * 512:(tg + 1) * 512],
                                lh, yh[:, kc * 1024 + tg * 512:kc * 1024 + (tg + 1) * 512],
                                start=(kc == 0), stop=(kc == 63))
                    last.then_inc(mm, 1)

        @blk.vector
        def _(e):
            for half in range(2):
                for mo in range(32):
                    it = half * 32 + mo
                    e.wait_ge(mm, it + 1)
                    e.wait_ge(xl, (it + 1) * 16)
                    if it >= 2:
                        e.wait_ge(stS, 16 * (it - 1))
                    e.tensor_add(x2b[:, (it % 2) * 1024:(it % 2 + 1) * 1024],
                                 x1b[:, (it % 2) * 1024:(it % 2 + 1) * 1024],
                                 pps[it % 2][:, :]).then_inc(ev, 1)

        @blk.scalar
        def _(e):
            n = 0
            for half in range(2):
                for mo in range(32):
                    it = half * 32 + mo
                    e.wait_ge(ev, it + 1)
                    e.dma_start(out=t["x2_sp"][mo * 128:(mo + 1) * 128,
                                               half * 1024:(half + 1) * 1024],
                                in_=x2b[:, (it % 2) * 1024:(it % 2 + 1) * 1024]
                                ).then_inc(stS, 16)
                    n += 16
            e.wait_ge(stS, n)



def kernel(**inputs):
    d = host_prepare(inputs)
    nc = build_program()
    shared = {k: v for k, v in d.items() if not k.startswith("_")}
    in_maps = []
    for c in range(8):
        m = dict(shared)
        m["x0t"] = d["_x0t"][c]
        m["oddm"] = np.full((128, 1), float(c % 2), dtype=np.float32)
        in_maps.append(m)
    res = run_bass_kernel_spmd(nc, in_maps, core_ids=list(range(8)))
    out = np.empty((Bsz, Lseq, DIM), dtype=np.float32)
    for c in range(8):
        b, half = c // 2, c % 2
        out[b, half * T:(half + 1) * T, :] = res.results[c]["outt"].T
    return out
